# version 13
# baseline (speedup 1.0000x reference)
import numpy as np
import ml_dtypes
import concourse.bass as bass
import concourse.mybir as mybir
import concourse.bass_utils as _bass_utils
from concourse.bass_utils import run_bass_kernel_spmd

if not getattr(_bass_utils, "_ldw_opt_patched", False):
    _orig_run_command = _bass_utils.run_command

    def _run_command_ldw(cmd, **kw):
        cmd = [c.replace("--enable-ldw-opt=false", "--enable-ldw-opt=true")
               if isinstance(c, str) else c for c in cmd]
        return _orig_run_command(cmd, **kw)

    _bass_utils.run_command = _run_command_ldw
    _bass_utils._ldw_opt_patched = True

F32 = mybir.dt.float32
BF16 = mybir.dt.bfloat16
AF = mybir.ActivationFunctionType
ALU = mybir.AluOpType
AX = mybir.AxisListType.X

NCORES = 8
T = 8192
C = 2048
KC = 16
NH = 16
DH = 128
HID = 8192
NJT = HID // 128
TC = 1024
NEG_BIG = -30000.0
EPS = 1e-5
ATT_SCALE = float(DH) ** -0.5

ENGS = ("t", "v", "a", "s", "g")
DMA_SEMS = ("sd", "gd", "ws", "xa0", "xa1", "xb0", "xb1", "q0", "q1",
            "k0", "k1", "v0", "v1", "hl0", "hl1", "vt0", "vt1", "rs",
            "wv0", "wv1", "wc0", "wc1", "wf0", "wf1")
SEMS = DMA_SEMS + ("sy", "gy", "tm", "vv", "aa", "ccG", "ccG2", "ccA",
        "ccB", "ccC", "ccD")
COMPUTE_SEMS = ("tm", "vv", "aa")


class B:

    def __init__(self, nc, sems):
        self.nc = nc
        self.sems = sems
        self.ops = {e: [] for e in ENGS}
        self.cnt = {s: 0 for s in SEMS}
        self.qsems = {"s": set(["sd"]), "g": set(["gd"])}

    def wait(self, eng, tok):
        if tok is None:
            return
        sem, val = tok
        if val <= 0:
            return
        self.ops[eng].append(("w", sem, val))

    def waits(self, eng, toks):
        for t in toks:
            self.wait(eng, t)

    def op(self, eng, fn, inc=None, amt=1, waits=()):
        self.waits(eng, waits)
        tok = None
        if inc is not None:
            self.cnt[inc] += amt
            tok = (inc, self.cnt[inc])
        self.ops[eng].append(("o", fn, inc, amt))
        return tok

    def dma(self, eng, out, in_, waits=(), sem=None):
        if sem is None:
            sem = "sd" if eng == "s" else "gd"
        else:
            self.qsems[eng].add(sem)
        return self.op(eng, lambda E, o=out, i=in_: E.dma_start(out=o, in_=i),
                       inc=sem, amt=16, waits=waits)

    def queue_barrier(self, eng):
        for s in sorted(self.qsems[eng]):
            self.wait(eng, (s, self.cnt[s]))

    def fence(self):
        self.queue_barrier("s")
        self.cnt["sy"] += 1
        self.ops["s"].append(
            ("o", lambda E: E.sem_inc(self.sems["sy"], 1), None, 0))
        self.queue_barrier("g")
        self.cnt["gy"] += 1
        self.ops["g"].append(
            ("o", lambda E: E.sem_inc(self.sems["gy"], 1), None, 0))
        toks = [("sy", self.cnt["sy"]), ("gy", self.cnt["gy"])]
        toks += [(s, self.cnt[s]) for s in COMPUTE_SEMS if self.cnt[s] > 0]
        for e in ENGS:
            self.waits(e, toks)

    def now(self, sem):
        return (sem, self.cnt[sem])

    def emit(self, block):
        for ename, attr in (("t", "tensor"), ("v", "vector"), ("a", "scalar"),
                            ("s", "sync"), ("g", "gpsimd")):
            ops_ = self.ops[ename]

            def body(E, ops_=ops_):
                for rec in ops_:
                    if rec[0] == "w":
                        E.wait_ge(self.sems[rec[1]], rec[2])
                    else:
                        _, fn, inc, amt = rec
                        inst = fn(E)
                        if inc is not None:
                            inst.then_inc(self.sems[inc], amt)

            getattr(block, attr)(body)


def build_program():
    nc = bass.Bass("TRN2", target_bir_lowering=False)

    xin = nc.dram_tensor("xin", [C, TC], F32, kind="ExternalInput").ap()
    wg = nc.dram_tensor("wg", [NJT * 128, KC * 128], F32, kind="ExternalInput").ap()
    wu = nc.dram_tensor("wu", [NJT * 128, KC * 128], F32, kind="ExternalInput").ap()
    wd = nc.dram_tensor("wd", [HID, C], F32, kind="ExternalInput").ap()
    wqh = nc.dram_tensor("wqh", [2 * C, DH], F32, kind="ExternalInput").ap()
    wkh = nc.dram_tensor("wkh", [2 * C, DH], F32, kind="ExternalInput").ap()
    wvh = nc.dram_tensor("wvh", [2 * C, DH], F32, kind="ExternalInput").ap()
    woh = nc.dram_tensor("woh", [2 * DH, C], F32, kind="ExternalInput").ap()
    bqh = nc.dram_tensor("bqh", [DH, 2], F32, kind="ExternalInput").ap()
    bkh = nc.dram_tensor("bkh", [DH, 2], F32, kind="ExternalInput").ap()
    bvh = nc.dram_tensor("bvh", [DH, 2], F32, kind="ExternalInput").ap()
    boT = nc.dram_tensor("boT", [DH, KC], F32, kind="ExternalInput").ap()
    identD = nc.dram_tensor("ident", [128, 128], F32, kind="ExternalInput").ap()
    onescD = nc.dram_tensor("onesc", [128, 1], F32, kind="ExternalInput").ap()
    onesrD = nc.dram_tensor("onesr", [1, 128], F32, kind="ExternalInput").ap()
    masktD = nc.dram_tensor("maskt", [512, 512], BF16, kind="ExternalInput").ap()
    sum4D = nc.dram_tensor("sum4", [97, 1], F32, kind="ExternalInput").ap()

    xb = nc.dram_tensor("xb", [2 * C, TC], BF16)
    xg = nc.dram_tensor("xg", [NCORES * 2 * C, TC], BF16, addr_space="Shared")
    qds = nc.dram_tensor("qds", [2 * 2 * DH, T], BF16)
    kds = nc.dram_tensor("kds", [2 * 2 * DH, T], BF16)
    vds = nc.dram_tensor("vds", [2 * T, DH], F32)
    wopH = [nc.dram_tensor("wop%d" % q, [NCORES * C, 512], F32)
            for q in range(2)]
    rsH = [nc.dram_tensor("rs%d" % q, [C, 512], F32) for q in range(2)]
    wgc = nc.dram_tensor("wgc", [NJT * 128, 2 * KC * 128], BF16)
    wuc = nc.dram_tensor("wuc", [NJT * 128, 2 * KC * 128], BF16)
    out = nc.dram_tensor("out", [C, TC], F32, kind="ExternalOutput").ap()

    x_res = nc.alloc_sbuf_tensor("x_res", [128, KC * TC], F32).ap()
    pool_ln = nc.alloc_sbuf_tensor("pool_ln", [128, 2 * KC * TC], BF16).ap()
    xln_hi = pool_ln[:, 0:KC * TC]
    xln_lo = pool_ln[:, KC * TC:2 * KC * TC]
    POOLN = 19456
    pool = nc.alloc_sbuf_tensor("pool", [128, POOLN], F32).ap()
    negm_b = pool[:, 3 * TC:4 * TC]
    r_b = pool[:, 4 * TC:5 * TC]
    ident = nc.alloc_sbuf_tensor("identt", [128, 128], F32).ap()
    onesc = nc.alloc_sbuf_tensor("onesct", [128, 1], F32).ap()
    onesr = nc.alloc_sbuf_tensor("onesrt", [1, 128], F32).ap()
    sum4 = nc.alloc_sbuf_tensor("sum4t", [97, 1], F32).ap()
    maskt = pool[:, 9216:9216 + 1024].bitcast(BF16)
    biases = nc.alloc_sbuf_tensor("biases", [128, 8 + KC], F32).ap()
    bq_s, bk_s, bv_s = biases[:, 0:2], biases[:, 2:4], biases[:, 4:6]
    bneg_s = biases[:, 6:8]
    bo_s = biases[:, 8:8 + KC]
    smalls = nc.alloc_sbuf_tensor("smalls", [1, 32], F32).ap()

    def slot_f32(off, n):
        return pool[:, off:off + n]

    def slot_bf16(off, n):
        return pool[:, off:off + n // 2].bitcast(BF16)

    pb = [nc.alloc_psum_tensor("pb%d" % i, [128, 512], F32).ap()
          for i in range(8)]

    sems = {s: nc.alloc_semaphore(s) for s in SEMS}
    bld = B(nc, sems)

    def mm(lout, lhsT, rhs, start, stop, waits=(), inc=None,
           tile_position=None):
        return bld.op("t", lambda E, o=lout, a=lhsT, b=rhs, s1=start, s2=stop,
                      tp=tile_position:
                      E.matmul(o, a, b, start=s1, stop=s2, tile_position=tp),
                      inc=inc, waits=waits)

    def tr(lout, in_, waits=(), inc=None):
        return bld.op("t", lambda E, o=lout, i=in_:
                      E.transpose(o, i, ident), inc=inc, waits=waits)

    def v_op(fn, waits=(), inc="vv"):
        return bld.op("v", fn, inc=inc, waits=waits)

    def a_op(fn, waits=(), inc="aa"):
        return bld.op("a", fn, inc=inc, waits=waits)

    bld.dma("s", ident[:, :], identD)
    bld.dma("s", onesc[:, :], onescD)
    bld.dma("s", onesr[:, :], onesrD)
    bld.dma("s", sum4[:, :], sum4D)
    bld.dma("s", bq_s, bqh)
    bld.dma("s", bk_s, bkh)
    bld.dma("s", bv_s, bvh)
    bld.dma("s", bo_s, boT)
    for kc in range(KC):
        bld.dma("s", x_res[:, kc * TC:(kc + 1) * TC],
                xin[kc * 128:(kc + 1) * 128, :])

    def ln_full():
        bld.fence()
        sq = [slot_f32(0, TC), slot_f32(TC, TC)]
        xlnf = slot_f32(2 * TC, TC)
        s1p = [pb[6][0:1, :], pb[7][0:1, :]]
        s2p = [pb[4][0:1, :], pb[5][0:1, :]]
        sq_user = [None, None]
        for kc in range(KC):
            xc = x_res[:, kc * TC:(kc + 1) * TC]
            sq_tok = a_op(lambda E, o=sq[kc % 2], i=xc:
                          E.activation(o, i, AF.Square),
                          waits=[sq_user[kc % 2]])
            for h in range(2):
                xch = x_res[:, kc * TC + h * 512: kc * TC + h * 512 + 512]
                mm(s1p[h], onesc, xch, start=(kc == 0), stop=(kc == KC - 1))
                sqh = sq[kc % 2][:, h * 512:(h + 1) * 512]
                tok = mm(s2p[h], onesc, sqh, start=(kc == 0),
                         stop=(kc == KC - 1),
                         waits=[sq_tok] if h == 0 else (),
                         inc="tm" if h == 1 else None)
            sq_user[kc % 2] = tok
        stats_tok = sq_user[(KC - 1) % 2]
        R = 5 * TC
        negm_r = pool[0:1, R:R + TC]
        vpe_r = pool[0:1, R + TC:R + 2 * TC]
        r0_r = pool[0:1, R + 2 * TC:R + 3 * TC]
        t1_r = pool[0:1, R + 3 * TC:R + 4 * TC]
        r_r = pool[0:1, R + 4 * TC:R + 5 * TC]
        for h in range(2):
            sl = slice(h * 512, h * 512 + 512)
            v_op(lambda E, o=negm_r[:, sl], i=s1p[h]:
                 E.tensor_scalar_mul(o, i, -1.0 / C),
                 waits=[stats_tok] if h == 0 else ())
            v_op(lambda E, o=vpe_r[:, sl], i=s2p[h]:
                 E.tensor_scalar_mul(o, i, 1.0 / C))
        v_op(lambda E: E.tensor_mul(t1_r, negm_r, negm_r))
        v_op(lambda E: E.tensor_sub(vpe_r, vpe_r, t1_r))
        vpe_tok = v_op(lambda E: E.tensor_scalar_add(vpe_r, vpe_r, EPS))
        sqrt_tok = a_op(lambda E: E.activation(r0_r, vpe_r, AF.Sqrt),
                        waits=[vpe_tok])
        v_op(lambda E: E.reciprocal(r0_r, r0_r), waits=[sqrt_tok])
        v_op(lambda E: E.tensor_mul(t1_r, r0_r, r0_r))
        v_op(lambda E: E.tensor_mul(t1_r, t1_r, vpe_r))
        v_op(lambda E: E.tensor_scalar(t1_r, t1_r, -0.5, 1.5, op0=ALU.mult,
                                       op1=ALU.add))
        r_tok = v_op(lambda E: E.tensor_mul(r_r, r0_r, t1_r))
        bc_tok = None
        for h in range(2):
            sl = slice(h * 512, h * 512 + 512)
            mm(pb[4][:, :], onesr, negm_r[:, sl], start=True, stop=True,
               waits=[r_tok] if h == 0 else [bc_tok])
            t_bc = mm(pb[5][:, :], onesr, r_r[:, sl], start=True, stop=True,
                      inc="tm")
            v_op(lambda E, o=negm_b[:, sl], i=pb[4]: E.tensor_copy(o, i),
                 waits=[t_bc])
            bc_tok = v_op(lambda E, o=r_b[:, sl], i=pb[5]: E.tensor_copy(o, i))
        prev = bc_tok
        ln_toks = []
        for kc in range(KC):
            xc = x_res[:, kc * TC:(kc + 1) * TC]
            hi_c = xln_hi[:, kc * TC:(kc + 1) * TC]
            lo_c = xln_lo[:, kc * TC:(kc + 1) * TC]
            v_op(lambda E, o=xlnf, i=xc: E.tensor_add(o, i, negm_b),
                 waits=[prev])
            tmul = v_op(lambda E, o=xlnf: E.tensor_mul(o, o, r_b))
            hi_t = a_op(lambda E, o=hi_c, i=xlnf: E.activation(o, i, AF.Copy),
                        waits=[tmul])
            prev = v_op(lambda E, o=lo_c, i=xlnf, hh=hi_c:
                        E.tensor_tensor(o, i, hh, op=ALU.subtract),
                        waits=[hi_t])
            ln_toks.append((hi_t, prev))
        return ln_toks

    def ffn_phase(build_cache, store_out=False):
        bld.fence()
        silu_s = [slot_f32(0, TC), slot_f32(TC, TC)]
        gatedf = slot_f32(2 * TC, TC)
        wstage = slot_f32(3072, 2048)
        wout = [slot_bf16(5120, 4096), slot_bf16(7168, 4096)]
        wbuf = {"g": slot_bf16(9216, 4096), "u": slot_bf16(11264, 4096)}
        wdh_s = slot_bf16(13312, 4096)
        wdl_s = slot_bf16(15360, 4096)
        gated_hi = slot_bf16(17408, 2048)
        gated_lo = slot_bf16(18432, 2048)
        assert POOLN >= 19456

        wstage_users = [None, None]
        wout_user = [0, 0]
        wout_tok = [None, None]
        wc_toks = {}
        wd_free = None
        wd_ready = [None, None]
        gated_ready = [None, None]
        gated_free = [None, None]
        bank_free = {i: None for i in range(8)}
        wbuf_free = {"g": None, "u": None}
        npiece = [0]

        def split_weight(dst_hi, dst_lo, src_ap, extra_waits):
            d_tok = bld.dma("s", wstage, src_ap,
                            waits=[wstage_users[0], wstage_users[1]]
                            + list(extra_waits), sem="ws")
            h_tok = a_op(lambda E, oh=dst_hi, i=wstage:
                         E.activation(oh, i, AF.Copy), waits=[d_tok])
            l_tok = v_op(lambda E, ol=dst_lo, i=wstage, hh=dst_hi:
                         E.tensor_tensor(ol, i, hh, op=ALU.subtract),
                         waits=[h_tok])
            wstage_users[0] = h_tok
            wstage_users[1] = l_tok
            return h_tok, l_tok

        def w_piece(w, jt):
            slot = npiece[0] % 2
            npiece[0] += 1
            wsrc, wdst = (wg, wgc) if w == "g" else (wu, wuc)
            d_tok = bld.dma("s", wstage, wsrc[jt * 128:(jt + 1) * 128, :],
                            waits=[wstage_users[0], wstage_users[1]],
                            sem="ws")
            h_tok = a_op(lambda E, oh=wout[slot][:, 0:2048], i=wstage:
                         E.activation(oh, i, AF.Copy),
                         waits=[d_tok, wout_tok[slot]])
            l_tok = v_op(lambda E, ol=wout[slot][:, 2048:4096], i=wstage,
                         hh=wout[slot][:, 0:2048]:
                         E.tensor_tensor(ol, i, hh, op=ALU.subtract),
                         waits=[h_tok])
            wstage_users[0] = h_tok
            wstage_users[1] = l_tok
            od = bld.dma("g", wdst.ap()[jt * 128:(jt + 1) * 128, :],
                         wout[slot], waits=[l_tok], sem="wc%d" % slot)
            wout_tok[slot] = od
            wc_toks[(w, jt)] = od

        def fetch(w, jt):
            wsrcc = wgc if w == "g" else wuc
            sem = "wf0" if w == "g" else "wf1"
            tok = bld.dma("s", wbuf[w],
                          wsrcc.ap()[jt * 128:(jt + 1) * 128, :],
                          waits=[wbuf_free[w], wc_toks.get((w, jt))],
                          sem=sem)
            return tok

        def pass3(banks, whole, xh_f, xl_f, first_waits):
            whole_hi = whole[:, 0:2048]
            whole_lo = whole[:, 2048:4096]
            last = None
            for kc in range(KC):
                whi = whole_hi[:, kc * 128:(kc + 1) * 128]
                wlo = whole_lo[:, kc * 128:(kc + 1) * 128]
                w0 = first_waits if kc == 0 else ()
                mm(banks[0], whi, xh_f(kc, 0), start=(kc == 0), stop=False,
                   waits=w0)
                mm(banks[1], whi, xh_f(kc, 1), start=(kc == 0), stop=False)
                mm(banks[0], whi, xl_f(kc, 0), start=False, stop=False)
                mm(banks[1], whi, xl_f(kc, 1), start=False, stop=False)
                mm(banks[0], wlo, xh_f(kc, 0), start=False, stop=(kc == KC - 1))
                last = mm(banks[1], wlo, xh_f(kc, 1), start=False,
                          stop=(kc == KC - 1),
                          inc="tm" if kc == KC - 1 else None)
            return last

        def xh_f(kc, t):
            return xln_hi[:, kc * TC + t * 512: kc * TC + t * 512 + 512]

        def xl_f(kc, t):
            return xln_lo[:, kc * TC + t * 512: kc * TC + t * 512 + 512]

        def gated_evac(jj, gsel, gbk, ubk, g_tok, u_tok):
            sil = silu_s[jj]
            st0 = a_op(lambda E, o=sil[:, 0:512], i=gbk[0]:
                       E.activation(o, i, AF.Silu),
                       waits=[g_tok, gated_free[jj]])
            st1 = a_op(lambda E, o=sil[:, 512:1024], i=gbk[1]:
                       E.activation(o, i, AF.Silu))
            bank_free[0 + 2 * gsel] = st0
            bank_free[1 + 2 * gsel] = st1
            mt0 = v_op(lambda E, o=gatedf[:, 0:512], i0=sil[:, 0:512],
                       i1=ubk[0]: E.tensor_tensor(o, i0, i1, op=ALU.mult),
                       waits=[st0, u_tok])
            mt1 = v_op(lambda E, o=gatedf[:, 512:1024],
                       i0=sil[:, 512:1024], i1=ubk[1]:
                       E.tensor_tensor(o, i0, i1, op=ALU.mult),
                       waits=[st1])
            bank_free[4 + 2 * gsel] = mt1
            bank_free[5 + 2 * gsel] = mt1
            g_hi = gated_hi[:, jj * TC:(jj + 1) * TC]
            g_lo = gated_lo[:, jj * TC:(jj + 1) * TC]
            gh_t = a_op(lambda E, o=g_hi, i=gatedf:
                        E.activation(o, i, AF.Copy), waits=[mt1])
            gl_t = v_op(lambda E, o=g_lo, i=gatedf, hh=g_hi:
                        E.tensor_tensor(o, i, hh, op=ALU.subtract),
                        waits=[gh_t])
            gated_ready[jj] = (gh_t, gl_t)

        n_round = NJT // 2
        if build_cache:
            for r0 in range(2):
                for jt in (2 * r0, 2 * r0 + 1):
                    w_piece("g", jt)
                    w_piece("u", jt)
        for rnd in range(n_round):
            if build_cache and rnd + 2 < n_round:
                for jt in (2 * (rnd + 2), 2 * (rnd + 2) + 1):
                    w_piece("g", jt)
                    w_piece("u", jt)
            jts = (rnd * 2, rnd * 2 + 1)
            gsel = [jt % 2 for jt in jts]
            gb_ = [[pb[0 + 2 * g], pb[1 + 2 * g]] for g in gsel]
            ub_ = [[pb[4 + 2 * g], pb[5 + 2 * g]] for g in gsel]
            fg0 = fetch("g", jts[0])
            g_tok0 = pass3(gb_[0], wbuf["g"], xh_f, xl_f,
                           [bank_free[0 + 2 * gsel[0]],
                            bank_free[1 + 2 * gsel[0]], fg0])
            wbuf_free["g"] = g_tok0
            fu0 = fetch("u", jts[0])
            u_tok0 = pass3(ub_[0], wbuf["u"], xh_f, xl_f,
                           [bank_free[4 + 2 * gsel[0]],
                            bank_free[5 + 2 * gsel[0]], fu0])
            wbuf_free["u"] = u_tok0
            fg1 = fetch("g", jts[1])
            fu1 = fetch("u", jts[1])
            dh0, dl0 = split_weight(
                wdh_s[:, 0:2048], wdl_s[:, 0:2048],
                wd[jts[0] * 128:(jts[0] + 1) * 128, :], [wd_free])
            wd_ready[0] = (dh0, dl0)
            gated_evac(0, gsel[0], gb_[0], ub_[0], g_tok0, u_tok0)
            g_tok1 = pass3(gb_[1], wbuf["g"], xh_f, xl_f,
                           [bank_free[0 + 2 * gsel[1]],
                            bank_free[1 + 2 * gsel[1]], fg1])
            wbuf_free["g"] = g_tok1
            u_tok1 = pass3(ub_[1], wbuf["u"], xh_f, xl_f,
                           [bank_free[4 + 2 * gsel[1]],
                            bank_free[5 + 2 * gsel[1]], fu1])
            wbuf_free["u"] = u_tok1
            dh1, dl1 = split_weight(
                wdh_s[:, 2048:4096], wdl_s[:, 2048:4096],
                wd[jts[1] * 128:(jts[1] + 1) * 128, :], [wd_free])
            wd_ready[1] = (dh1, dl1)
            gated_evac(1, gsel[1], gb_[1], ub_[1], g_tok1, u_tok1)
            last_down = None
            ev_hist = [None, None]
            jj1_waited = False
            for dt in range(KC):
                dsel = dt % 2
                db = [pb[0 + 2 * dsel], pb[1 + 2 * dsel]]
                if dt < 2:
                    first = ([bank_free[0 + 2 * dsel], bank_free[1 + 2 * dsel]]
                             + list(gated_ready[0]) + list(wd_ready[0]))
                else:
                    first = list(ev_hist[dsel])
                for jj in range(2):
                    whi = wdh_s[:, jj * 2048 + dt * 128:
                                jj * 2048 + (dt + 1) * 128]
                    wlo = wdl_s[:, jj * 2048 + dt * 128:
                                jj * 2048 + (dt + 1) * 128]
                    gh = [gated_hi[:, jj * TC + t * 512:
                                   jj * TC + t * 512 + 512] for t in range(2)]
                    gl = [gated_lo[:, jj * TC + t * 512:
                                   jj * TC + t * 512 + 512] for t in range(2)]
                    if jj == 0:
                        w0 = first
                    elif not jj1_waited:
                        w0 = list(gated_ready[1]) + list(wd_ready[1])
                        jj1_waited = True
                    else:
                        w0 = ()
                    mm(db[0], whi, gh[0], start=(jj == 0), stop=False,
                       waits=w0)
                    mm(db[1], whi, gh[1], start=(jj == 0), stop=False)
                    mm(db[0], whi, gl[0], start=False, stop=False)
                    mm(db[1], whi, gl[1], start=False, stop=False)
                    mm(db[0], wlo, gh[0], start=False, stop=(jj == 1))
                    last_down = mm(db[1], wlo, gh[1], start=False,
                                   stop=(jj == 1),
                                   inc="tm" if jj == 1 else None)
                e0 = v_op(lambda E,
                          o=x_res[:, dt * TC:dt * TC + 512], i=db[0]:
                          E.scalar_tensor_tensor(o, i, 0.5, o, op0=ALU.mult,
                                                 op1=ALU.add),
                          waits=[last_down])
                e1 = v_op(lambda E,
                          o=x_res[:, dt * TC + 512:dt * TC + 1024], i=db[1]:
                          E.scalar_tensor_tensor(o, i, 0.5, o, op0=ALU.mult,
                                                 op1=ALU.add))
                ev_hist[dsel] = (e0, e1)
                bank_free[0 + 2 * dsel] = e0
                bank_free[1 + 2 * dsel] = e1
                if store_out and rnd == n_round - 1:
                    bld.dma("g", out[dt * 128:(dt + 1) * 128, :],
                            x_res[:, dt * TC:(dt + 1) * TC],
                            waits=[e0, e1])
            gated_free[0] = last_down
            gated_free[1] = last_down
            wd_free = last_down
        bld.fence()

    ln_full()
    ffn_phase(build_cache=True)

    ln_toks = ln_full()
    for kc in range(KC):
        hi_t, lo_t = ln_toks[kc]
        bld.dma("g", xb.ap()[kc * 128:(kc + 1) * 128, :],
                xln_hi[:, kc * TC:(kc + 1) * TC], waits=[hi_t, lo_t])
        bld.dma("g", xb.ap()[C + kc * 128:C + (kc + 1) * 128, :],
                xln_lo[:, kc * TC:(kc + 1) * TC])
    bld.queue_barrier("g")
    ag_tok = bld.op("g", lambda E: E.collective_compute(
        "AllGather", ALU.bypass, replica_groups=[list(range(NCORES))],
        ins=[xb.ap()], outs=[xg.ap()]), inc="ccG")

    def qkv_phase():
        NT = 32
        TW = 256
        whl = []
        o = 0
        for m in range(6):
            if m % 3 == 2:
                base = 24576 + (m // 3) * 4096
                whl.append((pool_ln[:, base:base + 2048],
                            pool_ln[:, base + 2048:base + 4096]))
            else:
                whl.append((slot_bf16(o, 2048), slot_bf16(o + 1024, 2048)))
                o += 2048
        xgb = []
        for b_ in range(2):
            xgb.append((slot_bf16(o, KC * TW), slot_bf16(o + KC * TW // 2,
                                                         KC * TW)))
            o += KC * TW
        evacf = [slot_f32(o, TW), slot_f32(o + TW, TW)]; o += 2 * TW
        hb = [slot_bf16(o, TW), slot_bf16(o + TW // 2, TW)]; o += TW
        lb = [slot_bf16(o, TW), slot_bf16(o + TW // 2, TW)]; o += TW
        sqs = slot_f32(o, TW); o += TW
        vt_s = [slot_f32(o, TW), slot_f32(o + TW, TW)]; o += 2 * TW
        assert o <= POOLN, o
        wstage = pool[:, 12288:12288 + 2048]
        wstage3 = wstage.rearrange("p (kc d) -> p kc d", kc=KC)

        bld.fence()
        wsrc = (wqh, wkh, wvh)
        ws_users = [None, None]
        wready = []
        for m in range(6):
            hh, m_ = m // 3, m % 3
            src = wsrc[m_][hh * C:(hh + 1) * C, :]
            src = src.rearrange("(kc p) d -> p kc d", p=128)
            d_tok = bld.dma("s", wstage3, src, waits=ws_users, sem="ws")
            h_tok = a_op(lambda E, oh=whl[m][0], i=wstage:
                         E.activation(oh, i, AF.Copy), waits=[d_tok])
            l_tok = v_op(lambda E, ol=whl[m][1], i=wstage, hx=whl[m][0]:
                         E.tensor_tensor(ol, i, hx, op=ALU.subtract),
                         waits=[h_tok])
            ws_users = [h_tok, l_tok]
            wready.append((h_tok, l_tok))
        init_tok = v_op(lambda E: E.memset(smalls[:, 0:4], 0.0))

        bank_user = {i: None for i in range(8)}
        evac_user = [None, None]
        hl_user = [None, None]
        vt_user = [None, None]
        xg_user = [ws_users[0], ws_users[1]]
        stats_user = None
        nb = 0
        for tt in range(NT):
            r, colh = tt // 4, tt % 4
            b_ = tt % 2
            cs = slice(colh * TW, (colh + 1) * TW)
            src_h = xg.ap()[r * 2 * C:r * 2 * C + C, cs]
            src_l = xg.ap()[r * 2 * C + C:r * 2 * C + 2 * C, cs]
            sem = "xa%d" % b_
            dh_tok = bld.dma("s",
                             xgb[b_][0].rearrange("p (kc n) -> p kc n", kc=KC),
                             src_h.rearrange("(kc p) n -> p kc n", p=128),
                             waits=[ag_tok, xg_user[b_]], sem=sem)
            dl_tok = bld.dma("s",
                             xgb[b_][1].rearrange("p (kc n) -> p kc n", kc=KC),
                             src_l.rearrange("(kc p) n -> p kc n", p=128),
                             sem=sem)
            for h in range(2):
                for m_ in range(3):
                    m = h * 3 + m_
                    banki = (tt % 2) * 2 + h
                    bank = pb[banki][:, 0:TW]
                    first = [dh_tok, dl_tok, wready[m][0], wready[m][1],
                             bank_user[banki], init_tok]
                    tok = None
                    for kc in range(KC):
                        whi = whl[m][0][:, kc * 128:(kc + 1) * 128]
                        wlo = whl[m][1][:, kc * 128:(kc + 1) * 128]
                        xh = xgb[b_][0][:, kc * TW:(kc + 1) * TW]
                        xl = xgb[b_][1][:, kc * TW:(kc + 1) * TW]
                        w0 = first if kc == 0 else ()
                        mm(bank, whi, xh, start=(kc == 0), stop=False, waits=w0)
                        mm(bank, whi, xl, start=False, stop=False)
                        tok = mm(bank, wlo, xh, start=False,
                                 stop=(kc == KC - 1),
                                 inc="tm" if kc == KC - 1 else None)
                    xg_user[b_] = tok
                    bias_ap = (bq_s, bk_s, bv_s)[m_][:, h:h + 1]
                    eb = nb % 2
                    ev = v_op(lambda E, o=evacf[eb], i=bank, ba=bias_ap:
                              E.tensor_scalar_add(o, i, ba),
                              waits=[tok, evac_user[eb]])
                    bank_user[banki] = ev
                    if m_ in (0, 1):
                        dst = qds if m_ == 0 else kds
                        hi_t = a_op(lambda E, o=hb[eb], i=evacf[eb]:
                                    E.activation(o, i, AF.Copy),
                                    waits=[ev, hl_user[eb]])
                        lo_t = v_op(lambda E, o=lb[eb], i=evacf[eb], hx=hb[eb]:
                                    E.tensor_tensor(o, i, hx, op=ALU.subtract),
                                    waits=[hi_t])
                        bld.dma("g", dst.ap()[h * 256:h * 256 + 128,
                                              tt * TW:(tt + 1) * TW],
                                hb[eb], waits=[hi_t], sem="hl%d" % eb)
                        d2 = bld.dma("g",
                                     dst.ap()[h * 256 + 128:h * 256 + 256,
                                              tt * TW:(tt + 1) * TW],
                                     lb[eb], waits=[lo_t], sem="hl%d" % eb)
                        hl_user[eb] = d2
                        sq_t = a_op(lambda E, o=sqs, i=evacf[eb]:
                                    E.activation(o, i, AF.Square), waits=[ev])
                        n2 = mm(pb[6][0:1, 0:TW], onesc, sqs, start=True,
                                stop=True, waits=[sq_t, stats_user], inc="tm")
                        rd = v_op(lambda E, o=smalls[:, 8:9],
                                  i=pb[6][0:1, 0:TW]:
                                  E.reduce_max(o, i, axis=AX), waits=[n2])
                        acc = m_ * 2 + h
                        mx = v_op(lambda E, o=smalls[:, acc:acc + 1],
                                  i=smalls[:, 8:9]: E.tensor_max(o, o, i))
                        stats_user = mx
                        evac_user[eb] = sq_t
                    else:
                        vb = nb % 2
                        tts = None
                        for ti in range(2):
                            tts = tr(pb[7][:, ti * 128:(ti + 1) * 128],
                                     evacf[eb][:, ti * 128:(ti + 1) * 128],
                                     waits=[ev, bank_user[7]] if ti == 0
                                     else (),
                                     inc="tm" if ti == 1 else None)
                        cp = v_op(lambda E, o=vt_s[vb], i=pb[7][:, 0:TW]:
                                  E.tensor_copy(o, i),
                                  waits=[tts, vt_user[vb]])
                        bank_user[7] = cp
                        dv = bld.dma("g",
                                     vds.ap()[h * T + tt * TW:
                                              h * T + (tt + 1) * TW, :]
                                     .rearrange("(a p) d -> p a d", p=128),
                                     vt_s[vb].rearrange("p (a d) -> p a d",
                                                        a=2),
                                     waits=[cp], sem="vt%d" % vb)
                        vt_user[vb] = dv
                        evac_user[eb] = cp
                    nb += 1
        bc_user = None
        for h in range(2):
            pr = v_op(lambda E, o=smalls[:, 16 + h:17 + h],
                      i0=smalls[:, 0 + h:1 + h], i1=smalls[:, 2 + h:3 + h]:
                      E.tensor_mul(o, i0, i1), waits=[stats_user])
            sq_t = a_op(lambda E, o=smalls[:, 16 + h:17 + h],
                        i=smalls[:, 16 + h:17 + h]:
                        E.activation(o, i, AF.Sqrt), waits=[pr])
            ng = v_op(lambda E, o=smalls[:, 16 + h:17 + h],
                      i=smalls[:, 16 + h:17 + h]:
                      E.tensor_scalar_mul(o, i, -ATT_SCALE), waits=[sq_t])
            bc = mm(pb[6][:, 0:1], onesr, smalls[:, 16 + h:17 + h],
                    start=True, stop=True, waits=[ng, bc_user], inc="tm")
            bc_user = v_op(lambda E, o=bneg_s[:, h:h + 1], i=pb[6][:, 0:1]:
                           E.tensor_copy(o, i), waits=[bc])
        bld.fence()

    qkv_phase()

    def attn_phase():
        mt_tok = bld.dma("s", maskt.rearrange("p (a n) -> p a n", a=4),
                         masktD.rearrange("(a p) n -> p a n", p=128))
        bld.wait("t", mt_tok)
        o = 0
        qh_s = [slot_bf16(o, 512), slot_bf16(o + 256, 512)]; o += 512
        ql_s = [slot_bf16(o, 512), slot_bf16(o + 256, 512)]; o += 512
        kh_s = [slot_bf16(o, 512), slot_bf16(o + 256, 512)]; o += 512
        kl_s = [slot_bf16(o, 512), slot_bf16(o + 256, 512)]; o += 512
        vt_l = [slot_f32(o, 512), slot_f32(o + 512, 512)]; o += 1024
        P_s = [slot_f32(o + i * 512, 512) for i in range(8)]; o += 8 * 512
        recipb = slot_f32(o, 512); o += 512
        yf = slot_f32(o, 512); o += 512
        recr = pool[0:1, o:o + 512]; o += 512
        cbuf = slot_f32(o, 512); o += 512
        assert o <= POOLN

        y_hi = pool_ln[:, 0:KC * TC]
        y_lo = pool_ln[:, KC * TC:2 * KC * TC]
        scpb = [pb[0], pb[1], pb[5], pb[6]]

        q_user = [None, None]
        k_user = [None, None]
        v_user = [None, None]
        p_user = [None] * 8
        sc_user = [None] * 4
        y_user = None
        sums_user = None
        rb_user = None
        yb_user = None
        np_ = 0

        def pv_sums_batch(batch, sb, qc):
            toks = []
            for j, (pi, pvt, p_tok, ptv) in enumerate(batch):
                w1 = [p_tok, ptv]
                if sb == 0 and j == 0:
                    w1 += [y_user, sums_user]
                mm(pb[2], pvt, P_s[pi], start=(sb == 0 and j == 0),
                   stop=(sb == qc and j == len(batch) - 1), waits=w1)
            for j, (pi, pvt, p_tok, ptv) in enumerate(batch):
                tok = mm(pb[3][32 * j:32 * j + 1, :], onesc, P_s[pi],
                         start=(sb == 0), stop=(sb == qc),
                         inc="tm" if j == len(batch) - 1 else None,
                         tile_position=(0, 32 * j))
                if tok:
                    for pi2, _, _, _ in batch:
                        p_user[pi2] = tok
            return tok

        for h in range(2):
            for qc in range(16):
                qb = qc % 2
                tq = bld.dma("s", qh_s[qb],
                             qds.ap()[h * 256:h * 256 + 128,
                                      qc * 512:(qc + 1) * 512],
                             waits=[q_user[qb]], sem="q%d" % qb)
                tq2 = bld.dma("s", ql_s[qb],
                              qds.ap()[h * 256 + 128:h * 256 + 256,
                                       qc * 512:(qc + 1) * 512],
                              sem="q%d" % qb)
                pend = None
                kq_last = {}
                v_last = {}
                for sb in range(qc + 1):
                    kb = sb % 2
                    k_user[kb] = kq_last.get(kb, k_user[kb])
                    v_user[kb] = v_last.get(kb, v_user[kb])
                    tk = bld.dma("s", kh_s[kb],
                                 kds.ap()[h * 256:h * 256 + 128,
                                          sb * 512:(sb + 1) * 512],
                                 waits=[k_user[kb]], sem="k%d" % kb)
                    tk2 = bld.dma("s", kl_s[kb],
                                  kds.ap()[h * 256 + 128:h * 256 + 256,
                                           sb * 512:(sb + 1) * 512],
                                  sem="k%d" % kb)
                    vb = sb % 2
                    tv = bld.dma("g",
                                 vt_l[vb].rearrange("p (a d) -> p a d", a=4),
                                 vds.ap()[h * T + sb * 512:
                                          h * T + (sb + 1) * 512, :]
                                 .rearrange("(a p) d -> p a d", p=128),
                                 waits=[v_user[vb]], sem="v%d" % vb)
                    batch = []
                    for si in range(4):
                        spb = scpb[si]
                        khi = kh_s[kb][:, si * 128:(si + 1) * 128]
                        klo = kl_s[kb][:, si * 128:(si + 1) * 128]
                        w0 = [tk, tk2, tq, tq2, sc_user[si]]
                        mm(spb, khi, qh_s[qb], start=True, stop=False,
                           waits=w0)
                        mm(spb, khi, ql_s[qb], start=False, stop=False)
                        qk_tok = mm(spb, klo, qh_s[qb], start=False,
                                    stop=True, inc="tm")
                        kq_last[kb] = qk_tok
                        if sb == qc:
                            mk = maskt[:, si * 512:(si + 1) * 512]
                            qk_tok = v_op(lambda E, o=spb, i=spb, mkk=mk:
                                          E.tensor_tensor(o, i, mkk,
                                                          op=ALU.add),
                                          waits=[qk_tok])
                        pi = np_ % 8
                        np_ += 1
                        p_tok = a_op(lambda E, o=P_s[pi], i=spb,
                                     ba=bneg_s[:, h:h + 1]:
                                     E.activation(o, i, AF.Exp, bias=ba,
                                                  scale=ATT_SCALE),
                                     waits=[qk_tok, p_user[pi]])
                        sc_user[si] = p_tok
                        batch.append((pi, vt_l[vb][:, si * 128:(si + 1) * 128],
                                      p_tok, tv))
                    if pend is not None:
                        tokp = pv_sums_batch(pend[0], pend[1], qc)
                        v_last[pend[1] % 2] = tokp
                    pend = (batch, sb)
                tokp = pv_sums_batch(pend[0], pend[1], qc)
                v_last[pend[1] % 2] = tokp
                k_user[0] = kq_last.get(0, k_user[0])
                k_user[1] = kq_last.get(1, k_user[1])
                v_user[0] = v_last.get(0, v_user[0])
                v_user[1] = v_last.get(1, v_user[1])
                q_user[qb] = tokp
                done_tok = tokp
                cp4 = v_op(lambda E: E.tensor_copy(cbuf[0:97, :],
                                                   pb[3][0:97, :]),
                           waits=[done_tok, rb_user])
                sums_user = cp4
                tot = mm(pb[4][0:1, :], sum4, cbuf[0:97, :], start=True,
                         stop=True, waits=[cp4, rb_user], inc="tm")
                rc = v_op(lambda E: E.reciprocal(recr, pb[4][0:1, :]),
                          waits=[tot])
                bc = mm(pb[4], onesr, recr, start=True, stop=True,
                        waits=[rc], inc="tm")
                cb = v_op(lambda E: E.tensor_copy(recipb, pb[4]), waits=[bc])
                rb_user = cb
                ys = v_op(lambda E: E.tensor_tensor(yf, pb[2], recipb,
                                                    op=ALU.mult),
                          waits=[cb, yb_user])
                y_user = ys
                dsty_h = y_hi[:, h * T + qc * 512: h * T + (qc + 1) * 512]
                dsty_l = y_lo[:, h * T + qc * 512: h * T + (qc + 1) * 512]
                yh_t = a_op(lambda E, o=dsty_h, i=yf:
                            E.activation(o, i, AF.Copy), waits=[ys])
                yb_user = v_op(lambda E, o=dsty_l, i=yf, hx=dsty_h:
                               E.tensor_tensor(o, i, hx, op=ALU.subtract),
                               waits=[yh_t])
        bld.fence()

    attn_phase()

    def wo_phase():
        wstage = slot_f32(3 * TC, 2048)
        o = 3 * TC + 2048
        wo_hi = slot_bf16(o, 2 * 2048); o += 2048
        wo_lo = slot_bf16(o, 2 * 2048); o += 2048
        wev = [slot_f32(o, 512), slot_f32(o + 512, 512)]; o += 1024
        rsb = slot_f32(o, 512); o += 512
        tmp = slot_f32(o, 512); o += 512
        assert o <= POOLN
        y_hi = pool_ln[:, 0:KC * TC]
        y_lo = pool_ln[:, KC * TC:2 * KC * TC]

        ws_users = [None, None]
        wr = []
        for cc in range(2):
            d_tok = bld.dma("s", wstage, woh[cc * 128:(cc + 1) * 128, :],
                            waits=ws_users, sem="ws")
            h_tok = a_op(lambda E, oh=wo_hi[:, cc * 2048:(cc + 1) * 2048],
                         i=wstage: E.activation(oh, i, AF.Copy),
                         waits=[d_tok])
            l_tok = v_op(lambda E, ol=wo_lo[:, cc * 2048:(cc + 1) * 2048],
                         i=wstage, hx=wo_hi[:, cc * 2048:(cc + 1) * 2048]:
                         E.tensor_tensor(ol, i, hx, op=ALU.subtract),
                         waits=[h_tok])
            ws_users = [h_tok, l_tok]
            wr.append((h_tok, l_tok))

        bank_user = [None, None]
        ev_user = [None, None]
        rs_toks = []
        xadd_state = [None, None]

        def x2_add(half):
            d_user, t_user = xadd_state
            for kc in range(KC):
                dr = bld.dma("s", rsb,
                             rsH[half].ap()[kc * 128:(kc + 1) * 128, :],
                             waits=[rs_toks[half], d_user], sem="rs")
                at = a_op(lambda E, o=tmp, i=rsb, ba=bo_s[:, kc:kc + 1]:
                          E.activation(o, i, AF.Identity, bias=ba),
                          waits=[dr, t_user])
                d_user = at
                t_user = v_op(lambda E,
                              o=x_res[:, kc * TC + half * 512:
                                      kc * TC + half * 512 + 512], i=tmp:
                              E.tensor_tensor(o, o, i, op=ALU.add),
                              waits=[at])
            xadd_state[0] = d_user
            xadd_state[1] = t_user

        for half in range(2):
            for r in range(NCORES):
                tt = 2 * r + half
                for dt in range(KC):
                    bi = (r * KC + dt) % 2
                    bank = pb[bi]
                    first = [bank_user[bi]] + list(wr[0]) + list(wr[1])
                    last = None
                    for cc in range(2):
                        whi = wo_hi[:, cc * 2048 + dt * 128:
                                    cc * 2048 + (dt + 1) * 128]
                        wlo = wo_lo[:, cc * 2048 + dt * 128:
                                    cc * 2048 + (dt + 1) * 128]
                        yh = y_hi[:, cc * T + tt * 512:cc * T + (tt + 1) * 512]
                        yl = y_lo[:, cc * T + tt * 512:cc * T + (tt + 1) * 512]
                        w0 = first if cc == 0 else ()
                        mm(bank, whi, yh, start=(cc == 0), stop=False,
                           waits=w0)
                        mm(bank, whi, yl, start=False, stop=False)
                        last = mm(bank, wlo, yh, start=False, stop=(cc == 1),
                                  inc="tm" if cc == 1 else None)
                    ev = v_op(lambda E, o=wev[bi], i=bank:
                              E.tensor_copy(o, i), waits=[last, ev_user[bi]])
                    bank_user[bi] = ev
                    ev_user[bi] = bld.dma(
                        "g",
                        wopH[half].ap()[r * C + dt * 128:
                                        r * C + (dt + 1) * 128, :],
                        wev[bi], waits=[ev], sem="wv%d" % bi)
            bld.queue_barrier("g")
            rs_toks.append(bld.op("g", lambda E, i=wopH[half], oo=rsH[half]:
                                  E.collective_compute(
                                      "ReduceScatter", ALU.add,
                                      replica_groups=[list(range(NCORES))],
                                      ins=[i.ap()], outs=[oo.ap()]),
                                  inc=("ccA", "ccB")[half]))
            if half == 1:
                x2_add(0)
                x2_add(1)
        bld.fence()

    wo_phase()

    ln_full()
    ffn_phase(build_cache=False, store_out=True)
    bld.queue_barrier("g")

    with nc.Block() as block:
        bld.emit(block)
    return nc


_NC_CACHE = [None]


def kernel(L0_attn_x, L0_attn_Wq, L0_attn_bq, L0_attn_Wk, L0_attn_bk,
           L0_attn_Wv, L0_attn_bv, L0_attn_Wo, L0_attn_bo,
           L0_ffn_w_gate, L0_ffn_w_up, L0_ffn_w_down):
    f32 = np.float32
    x2d = np.asarray(L0_attn_x, f32).reshape(T, C)
    xT = np.ascontiguousarray(x2d.T)
    def jt_tiled(wT):
        r = wT.reshape(KC, 128, NJT, 128).transpose(2, 1, 0, 3)
        return np.ascontiguousarray(r.reshape(NJT * 128, KC * 128))

    wg_h = jt_tiled(np.asarray(L0_ffn_w_gate, f32).T)
    wu_h = jt_tiled(np.asarray(L0_ffn_w_up, f32).T)
    wd_h = np.ascontiguousarray(np.asarray(L0_ffn_w_down, f32).T)
    Wq = np.asarray(L0_attn_Wq, f32)
    Wk = np.asarray(L0_attn_Wk, f32)
    Wv = np.asarray(L0_attn_Wv, f32)
    Wo = np.asarray(L0_attn_Wo, f32)
    bq = np.asarray(L0_attn_bq, f32)
    bk = np.asarray(L0_attn_bk, f32)
    bv = np.asarray(L0_attn_bv, f32)
    bo = np.asarray(L0_attn_bo, f32)

    identc = np.eye(128, dtype=f32)
    onescc = np.ones((128, 1), dtype=f32)
    onesrc = np.ones((1, 128), dtype=f32)
    s_idx = np.arange(512)[:, None]
    t_idx = np.arange(512)[None, :]
    maskc = np.where(s_idx <= t_idx, 0.0, NEG_BIG).astype(ml_dtypes.bfloat16)
    boTc = np.ascontiguousarray(bo.reshape(KC, 128).T)
    sum4c = np.zeros((97, 1), dtype=f32)
    sum4c[[0, 32, 64, 96], 0] = 1.0

    in_maps = []
    for c in range(NCORES):
        h0, h1 = 2 * c, 2 * c + 1
        im = {
            "xin": np.ascontiguousarray(xT[:, c * TC:(c + 1) * TC]),
            "wg": wg_h, "wu": wu_h, "wd": wd_h,
            "wqh": np.ascontiguousarray(
                np.concatenate([Wq[h0], Wq[h1]], axis=0)),
            "wkh": np.ascontiguousarray(
                np.concatenate([Wk[h0], Wk[h1]], axis=0)),
            "wvh": np.ascontiguousarray(
                np.concatenate([Wv[h0], Wv[h1]], axis=0)),
            "woh": np.ascontiguousarray(Wo[h0 * DH:(h1 + 1) * DH, :]),
            "bqh": np.ascontiguousarray(np.stack([bq[h0], bq[h1]], axis=1)),
            "bkh": np.ascontiguousarray(np.stack([bk[h0], bk[h1]], axis=1)),
            "bvh": np.ascontiguousarray(np.stack([bv[h0], bv[h1]], axis=1)),
            "boT": boTc,
            "ident": identc, "onesc": onescc, "onesr": onesrc,
            "maskt": maskc, "sum4": sum4c,
        }
        in_maps.append(im)

    if _NC_CACHE[0] is None:
        _NC_CACHE[0] = build_program()
    nc = _NC_CACHE[0]
    res = run_bass_kernel_spmd(nc, in_maps, core_ids=list(range(NCORES)))

    y = np.empty((T, C), dtype=f32)
    for c in range(NCORES):
        o = res.results[c]["out"]
        y[c * TC:(c + 1) * TC, :] = o.T
    return y.reshape(1, T, C)


# version 14 (final)
# speedup vs baseline: 1.0750x; 1.0750x over previous
import numpy as np
import ml_dtypes
import concourse.bass as bass
import concourse.mybir as mybir
import concourse.bass_utils as _bass_utils
from concourse.bass_utils import run_bass_kernel_spmd

if not getattr(_bass_utils, "_ldw_opt_patched", False):
    _orig_run_command = _bass_utils.run_command

    def _run_command_ldw(cmd, **kw):
        cmd = [c.replace("--enable-ldw-opt=false", "--enable-ldw-opt=true")
               if isinstance(c, str) else c for c in cmd]
        return _orig_run_command(cmd, **kw)

    _bass_utils.run_command = _run_command_ldw
    _bass_utils._ldw_opt_patched = True

F32 = mybir.dt.float32
BF16 = mybir.dt.bfloat16
AF = mybir.ActivationFunctionType
ALU = mybir.AluOpType
AX = mybir.AxisListType.X

NCORES = 8
T = 8192
C = 2048
KC = 16
NH = 16
DH = 128
HID = 8192
NJT = HID // 128
TC = 1024
NEG_BIG = -30000.0
EPS = 1e-5
ATT_SCALE = float(DH) ** -0.5

ENGS = ("t", "v", "a", "s", "g")
DMA_SEMS = ("sd", "gd", "ws", "xa0", "xa1", "xb0", "xb1", "q0", "q1",
            "k0", "k1", "v0", "v1", "hl0", "hl1", "vt0", "vt1", "rs",
            "wv0", "wv1", "wc0", "wc1", "wf0", "wf1", "wf2", "wf3")
SEMS = DMA_SEMS + ("sy", "gy", "tm", "vv", "aa", "ccG", "ccG2", "ccA",
        "ccB", "ccC", "ccD")
COMPUTE_SEMS = ("tm", "vv", "aa")


class B:

    def __init__(self, nc, sems):
        self.nc = nc
        self.sems = sems
        self.ops = {e: [] for e in ENGS}
        self.cnt = {s: 0 for s in SEMS}
        self.qsems = {"s": set(["sd"]), "g": set(["gd"])}

    def wait(self, eng, tok):
        if tok is None:
            return
        sem, val = tok
        if val <= 0:
            return
        self.ops[eng].append(("w", sem, val))

    def waits(self, eng, toks):
        for t in toks:
            self.wait(eng, t)

    def op(self, eng, fn, inc=None, amt=1, waits=()):
        self.waits(eng, waits)
        tok = None
        if inc is not None:
            self.cnt[inc] += amt
            tok = (inc, self.cnt[inc])
        self.ops[eng].append(("o", fn, inc, amt))
        return tok

    def dma(self, eng, out, in_, waits=(), sem=None):
        if sem is None:
            sem = "sd" if eng == "s" else "gd"
        else:
            self.qsems[eng].add(sem)
        return self.op(eng, lambda E, o=out, i=in_: E.dma_start(out=o, in_=i),
                       inc=sem, amt=16, waits=waits)

    def queue_barrier(self, eng):
        for s in sorted(self.qsems[eng]):
            self.wait(eng, (s, self.cnt[s]))

    def fence(self):
        self.queue_barrier("s")
        self.cnt["sy"] += 1
        self.ops["s"].append(
            ("o", lambda E: E.sem_inc(self.sems["sy"], 1), None, 0))
        self.queue_barrier("g")
        self.cnt["gy"] += 1
        self.ops["g"].append(
            ("o", lambda E: E.sem_inc(self.sems["gy"], 1), None, 0))
        toks = [("sy", self.cnt["sy"]), ("gy", self.cnt["gy"])]
        toks += [(s, self.cnt[s]) for s in COMPUTE_SEMS if self.cnt[s] > 0]
        for e in ENGS:
            self.waits(e, toks)

    def now(self, sem):
        return (sem, self.cnt[sem])

    def emit(self, block):
        for ename, attr in (("t", "tensor"), ("v", "vector"), ("a", "scalar"),
                            ("s", "sync"), ("g", "gpsimd")):
            ops_ = self.ops[ename]

            def body(E, ops_=ops_):
                for rec in ops_:
                    if rec[0] == "w":
                        E.wait_ge(self.sems[rec[1]], rec[2])
                    else:
                        _, fn, inc, amt = rec
                        inst = fn(E)
                        if inc is not None:
                            inst.then_inc(self.sems[inc], amt)

            getattr(block, attr)(body)


def build_program():
    nc = bass.Bass("TRN2", target_bir_lowering=False)

    xin = nc.dram_tensor("xin", [C, TC], F32, kind="ExternalInput").ap()
    wg = nc.dram_tensor("wg", [NJT * 128, KC * 128], F32, kind="ExternalInput").ap()
    wu = nc.dram_tensor("wu", [NJT * 128, KC * 128], F32, kind="ExternalInput").ap()
    wd = nc.dram_tensor("wd", [HID, C], F32, kind="ExternalInput").ap()
    wqh = nc.dram_tensor("wqh", [2 * C, DH], F32, kind="ExternalInput").ap()
    wkh = nc.dram_tensor("wkh", [2 * C, DH], F32, kind="ExternalInput").ap()
    wvh = nc.dram_tensor("wvh", [2 * C, DH], F32, kind="ExternalInput").ap()
    woh = nc.dram_tensor("woh", [2 * DH, C], F32, kind="ExternalInput").ap()
    bqh = nc.dram_tensor("bqh", [DH, 2], F32, kind="ExternalInput").ap()
    bkh = nc.dram_tensor("bkh", [DH, 2], F32, kind="ExternalInput").ap()
    bvh = nc.dram_tensor("bvh", [DH, 2], F32, kind="ExternalInput").ap()
    boT = nc.dram_tensor("boT", [DH, KC], F32, kind="ExternalInput").ap()
    identD = nc.dram_tensor("ident", [128, 128], F32, kind="ExternalInput").ap()
    onescD = nc.dram_tensor("onesc", [128, 1], F32, kind="ExternalInput").ap()
    onesrD = nc.dram_tensor("onesr", [1, 128], F32, kind="ExternalInput").ap()
    masktD = nc.dram_tensor("maskt", [512, 512], BF16, kind="ExternalInput").ap()
    sum4D = nc.dram_tensor("sum4", [97, 1], F32, kind="ExternalInput").ap()

    xb = nc.dram_tensor("xb", [2 * C, TC], BF16)
    xg = nc.dram_tensor("xg", [NCORES * 2 * C, TC], BF16, addr_space="Shared")
    qds = nc.dram_tensor("qds", [2 * 2 * DH, T], BF16)
    kds = nc.dram_tensor("kds", [2 * 2 * DH, T], BF16)
    vds = nc.dram_tensor("vds", [2 * T, DH], F32)
    wopH = [nc.dram_tensor("wop%d" % q, [NCORES * C, 512], F32)
            for q in range(2)]
    rsH = [nc.dram_tensor("rs%d" % q, [C, 512], F32) for q in range(2)]
    wgc = nc.dram_tensor("wgc", [NJT * 128, 2 * KC * 128], BF16)
    wuc = nc.dram_tensor("wuc", [NJT * 128, 2 * KC * 128], BF16)
    wdc = nc.dram_tensor("wdc", [NJT * 128, 2 * C], BF16)
    out = nc.dram_tensor("out", [C, TC], F32, kind="ExternalOutput").ap()

    x_res = nc.alloc_sbuf_tensor("x_res", [128, KC * TC], F32).ap()
    pool_ln = nc.alloc_sbuf_tensor("pool_ln", [128, 2 * KC * TC], BF16).ap()
    xln_hi = pool_ln[:, 0:KC * TC]
    xln_lo = pool_ln[:, KC * TC:2 * KC * TC]
    POOLN = 19456
    pool = nc.alloc_sbuf_tensor("pool", [128, POOLN], F32).ap()
    negm_b = pool[:, 3 * TC:4 * TC]
    r_b = pool[:, 4 * TC:5 * TC]
    ident = nc.alloc_sbuf_tensor("identt", [128, 128], F32).ap()
    onesc = nc.alloc_sbuf_tensor("onesct", [128, 1], F32).ap()
    onesr = nc.alloc_sbuf_tensor("onesrt", [1, 128], F32).ap()
    sum4 = nc.alloc_sbuf_tensor("sum4t", [97, 1], F32).ap()
    maskt = pool[:, 9216:9216 + 1024].bitcast(BF16)
    biases = nc.alloc_sbuf_tensor("biases", [128, 8 + KC], F32).ap()
    bq_s, bk_s, bv_s = biases[:, 0:2], biases[:, 2:4], biases[:, 4:6]
    bneg_s = biases[:, 6:8]
    bo_s = biases[:, 8:8 + KC]
    smalls = nc.alloc_sbuf_tensor("smalls", [1, 32], F32).ap()

    def slot_f32(off, n):
        return pool[:, off:off + n]

    def slot_bf16(off, n):
        return pool[:, off:off + n // 2].bitcast(BF16)

    pb = [nc.alloc_psum_tensor("pb%d" % i, [128, 512], F32).ap()
          for i in range(8)]

    sems = {s: nc.alloc_semaphore(s) for s in SEMS}
    bld = B(nc, sems)

    def mm(lout, lhsT, rhs, start, stop, waits=(), inc=None,
           tile_position=None):
        return bld.op("t", lambda E, o=lout, a=lhsT, b=rhs, s1=start, s2=stop,
                      tp=tile_position:
                      E.matmul(o, a, b, start=s1, stop=s2, tile_position=tp),
                      inc=inc, waits=waits)

    def tr(lout, in_, waits=(), inc=None):
        return bld.op("t", lambda E, o=lout, i=in_:
                      E.transpose(o, i, ident), inc=inc, waits=waits)

    def v_op(fn, waits=(), inc="vv"):
        return bld.op("v", fn, inc=inc, waits=waits)

    def a_op(fn, waits=(), inc="aa"):
        return bld.op("a", fn, inc=inc, waits=waits)

    bld.dma("s", ident[:, :], identD)
    bld.dma("s", onesc[:, :], onescD)
    bld.dma("s", onesr[:, :], onesrD)
    bld.dma("s", sum4[:, :], sum4D)
    bld.dma("s", bq_s, bqh)
    bld.dma("s", bk_s, bkh)
    bld.dma("s", bv_s, bvh)
    bld.dma("s", bo_s, boT)
    for kc in range(KC):
        bld.dma("s", x_res[:, kc * TC:(kc + 1) * TC],
                xin[kc * 128:(kc + 1) * 128, :])

    def ln_full():
        bld.fence()
        sq = [slot_f32(0, TC), slot_f32(TC, TC)]
        xlnf = slot_f32(2 * TC, TC)
        s1p = [pb[6][0:1, :], pb[7][0:1, :]]
        s2p = [pb[4][0:1, :], pb[5][0:1, :]]
        sq_user = [None, None]
        for kc in range(KC):
            xc = x_res[:, kc * TC:(kc + 1) * TC]
            sq_tok = a_op(lambda E, o=sq[kc % 2], i=xc:
                          E.activation(o, i, AF.Square),
                          waits=[sq_user[kc % 2]])
            for h in range(2):
                xch = x_res[:, kc * TC + h * 512: kc * TC + h * 512 + 512]
                mm(s1p[h], onesc, xch, start=(kc == 0), stop=(kc == KC - 1))
                sqh = sq[kc % 2][:, h * 512:(h + 1) * 512]
                tok = mm(s2p[h], onesc, sqh, start=(kc == 0),
                         stop=(kc == KC - 1),
                         waits=[sq_tok] if h == 0 else (),
                         inc="tm" if h == 1 else None)
            sq_user[kc % 2] = tok
        stats_tok = sq_user[(KC - 1) % 2]
        R = 5 * TC
        negm_r = pool[0:1, R:R + TC]
        vpe_r = pool[0:1, R + TC:R + 2 * TC]
        r0_r = pool[0:1, R + 2 * TC:R + 3 * TC]
        t1_r = pool[0:1, R + 3 * TC:R + 4 * TC]
        r_r = pool[0:1, R + 4 * TC:R + 5 * TC]
        for h in range(2):
            sl = slice(h * 512, h * 512 + 512)
            v_op(lambda E, o=negm_r[:, sl], i=s1p[h]:
                 E.tensor_scalar_mul(o, i, -1.0 / C),
                 waits=[stats_tok] if h == 0 else ())
            v_op(lambda E, o=vpe_r[:, sl], i=s2p[h]:
                 E.tensor_scalar_mul(o, i, 1.0 / C))
        v_op(lambda E: E.tensor_mul(t1_r, negm_r, negm_r))
        v_op(lambda E: E.tensor_sub(vpe_r, vpe_r, t1_r))
        vpe_tok = v_op(lambda E: E.tensor_scalar_add(vpe_r, vpe_r, EPS))
        sqrt_tok = a_op(lambda E: E.activation(r0_r, vpe_r, AF.Sqrt),
                        waits=[vpe_tok])
        v_op(lambda E: E.reciprocal(r0_r, r0_r), waits=[sqrt_tok])
        v_op(lambda E: E.tensor_mul(t1_r, r0_r, r0_r))
        v_op(lambda E: E.tensor_mul(t1_r, t1_r, vpe_r))
        v_op(lambda E: E.tensor_scalar(t1_r, t1_r, -0.5, 1.5, op0=ALU.mult,
                                       op1=ALU.add))
        r_tok = v_op(lambda E: E.tensor_mul(r_r, r0_r, t1_r))
        bc_tok = None
        for h in range(2):
            sl = slice(h * 512, h * 512 + 512)
            mm(pb[4][:, :], onesr, negm_r[:, sl], start=True, stop=True,
               waits=[r_tok] if h == 0 else [bc_tok])
            t_bc = mm(pb[5][:, :], onesr, r_r[:, sl], start=True, stop=True,
                      inc="tm")
            v_op(lambda E, o=negm_b[:, sl], i=pb[4]: E.tensor_copy(o, i),
                 waits=[t_bc])
            bc_tok = v_op(lambda E, o=r_b[:, sl], i=pb[5]: E.tensor_copy(o, i))
        prev = bc_tok
        ln_toks = []
        for kc in range(KC):
            xc = x_res[:, kc * TC:(kc + 1) * TC]
            hi_c = xln_hi[:, kc * TC:(kc + 1) * TC]
            lo_c = xln_lo[:, kc * TC:(kc + 1) * TC]
            v_op(lambda E, o=xlnf, i=xc: E.tensor_add(o, i, negm_b),
                 waits=[prev])
            tmul = v_op(lambda E, o=xlnf: E.tensor_mul(o, o, r_b))
            hi_t = a_op(lambda E, o=hi_c, i=xlnf: E.activation(o, i, AF.Copy),
                        waits=[tmul])
            prev = v_op(lambda E, o=lo_c, i=xlnf, hh=hi_c:
                        E.tensor_tensor(o, i, hh, op=ALU.subtract),
                        waits=[hi_t])
            ln_toks.append((hi_t, prev))
        return ln_toks

    def ffn_phase(build_cache, store_out=False):
        bld.fence()
        silu_s = [slot_f32(0, TC), slot_f32(TC, TC)]
        gatedf = slot_f32(2 * TC, TC)
        wstage = slot_f32(3072, 2048)
        wout = [slot_bf16(5120, 4096), slot_bf16(7168, 4096)]
        wbuf = {"g": slot_bf16(9216, 4096), "u": slot_bf16(11264, 4096),
                "d0": slot_bf16(13312, 4096), "d1": slot_bf16(15360, 4096)}
        gated_hi = slot_bf16(17408, 2048)
        gated_lo = slot_bf16(18432, 2048)
        assert POOLN >= 19456

        wstage_users = [None, None]
        wout_tok = [None, None]
        wc_toks = {}
        wd_free = None
        gated_ready = [None, None]
        gated_free = [None, None]
        bank_free = {i: None for i in range(8)}
        wbuf_free = {"g": None, "u": None, "d0": None, "d1": None}
        npiece = [0]

        def w_piece(w, jt):
            slot = npiece[0] % 2
            npiece[0] += 1
            wsrc, wdst = {"g": (wg, wgc), "u": (wu, wuc), "d": (wd, wdc)}[w]
            half = wdst.shape[1] // 2
            d_tok = bld.dma("g", wstage, wsrc[jt * 128:(jt + 1) * 128, :],
                            waits=[wstage_users[0], wstage_users[1]],
                            sem="ws")
            h_tok = a_op(lambda E, oh=wout[slot][:, 0:half], i=wstage:
                         E.activation(oh, i, AF.Copy),
                         waits=[d_tok, wout_tok[slot]])
            l_tok = v_op(lambda E, ol=wout[slot][:, half:2 * half], i=wstage,
                         hh=wout[slot][:, 0:half]:
                         E.tensor_tensor(ol, i, hh, op=ALU.subtract),
                         waits=[h_tok])
            wstage_users[0] = h_tok
            wstage_users[1] = l_tok
            od = bld.dma("g", wdst.ap()[jt * 128:(jt + 1) * 128, :],
                         wout[slot][:, 0:2 * half], waits=[l_tok],
                         sem="wc%d" % slot)
            wout_tok[slot] = od
            wc_toks[(w, jt)] = od

        def fetch(key, w, jt):
            wsrcc = {"g": wgc, "u": wuc, "d": wdc}[w]
            sem = {"g": "wf0", "u": "wf1", "d0": "wf2", "d1": "wf3"}[key]
            tok = bld.dma("s", wbuf[key],
                          wsrcc.ap()[jt * 128:(jt + 1) * 128, :],
                          waits=[wbuf_free[key], wc_toks.get((w, jt))],
                          sem=sem)
            return tok

        def pass3(banks, whole, xh_f, xl_f, first_waits):
            whole_hi = whole[:, 0:2048]
            whole_lo = whole[:, 2048:4096]
            last = None
            for kc in range(KC):
                whi = whole_hi[:, kc * 128:(kc + 1) * 128]
                wlo = whole_lo[:, kc * 128:(kc + 1) * 128]
                w0 = first_waits if kc == 0 else ()
                mm(banks[0], whi, xh_f(kc, 0), start=(kc == 0), stop=False,
                   waits=w0)
                mm(banks[1], whi, xh_f(kc, 1), start=(kc == 0), stop=False)
                mm(banks[0], whi, xl_f(kc, 0), start=False, stop=False)
                mm(banks[1], whi, xl_f(kc, 1), start=False, stop=False)
                mm(banks[0], wlo, xh_f(kc, 0), start=False, stop=(kc == KC - 1))
                last = mm(banks[1], wlo, xh_f(kc, 1), start=False,
                          stop=(kc == KC - 1),
                          inc="tm" if kc == KC - 1 else None)
            return last

        def xh_f(kc, t):
            return xln_hi[:, kc * TC + t * 512: kc * TC + t * 512 + 512]

        def xl_f(kc, t):
            return xln_lo[:, kc * TC + t * 512: kc * TC + t * 512 + 512]

        def gated_evac(jj, gsel, gbk, ubk, g_tok, u_tok):
            sil = silu_s[jj]
            st0 = a_op(lambda E, o=sil[:, 0:512], i=gbk[0]:
                       E.activation(o, i, AF.Silu),
                       waits=[g_tok, gated_free[jj]])
            st1 = a_op(lambda E, o=sil[:, 512:1024], i=gbk[1]:
                       E.activation(o, i, AF.Silu))
            bank_free[0 + 2 * gsel] = st0
            bank_free[1 + 2 * gsel] = st1
            mt0 = v_op(lambda E, o=gatedf[:, 0:512], i0=sil[:, 0:512],
                       i1=ubk[0]: E.tensor_tensor(o, i0, i1, op=ALU.mult),
                       waits=[st0, u_tok])
            mt1 = v_op(lambda E, o=gatedf[:, 512:1024],
                       i0=sil[:, 512:1024], i1=ubk[1]:
                       E.tensor_tensor(o, i0, i1, op=ALU.mult),
                       waits=[st1])
            bank_free[4 + 2 * gsel] = mt1
            bank_free[5 + 2 * gsel] = mt1
            g_hi = gated_hi[:, jj * TC:(jj + 1) * TC]
            g_lo = gated_lo[:, jj * TC:(jj + 1) * TC]
            gh_t = a_op(lambda E, o=g_hi, i=gatedf:
                        E.activation(o, i, AF.Copy), waits=[mt1])
            gl_t = v_op(lambda E, o=g_lo, i=gatedf, hh=g_hi:
                        E.tensor_tensor(o, i, hh, op=ALU.subtract),
                        waits=[gh_t])
            gated_ready[jj] = (gh_t, gl_t)

        n_round = NJT // 2
        if build_cache:
            for r0 in range(2):
                for jt in (2 * r0, 2 * r0 + 1):
                    w_piece("g", jt)
                    w_piece("u", jt)
                    w_piece("d", jt)
        for rnd in range(n_round):
            if build_cache and rnd + 2 < n_round:
                for jt in (2 * (rnd + 2), 2 * (rnd + 2) + 1):
                    w_piece("g", jt)
                    w_piece("u", jt)
                    w_piece("d", jt)
            jts = (rnd * 2, rnd * 2 + 1)
            gsel = [jt % 2 for jt in jts]
            gb_ = [[pb[0 + 2 * g], pb[1 + 2 * g]] for g in gsel]
            ub_ = [[pb[4 + 2 * g], pb[5 + 2 * g]] for g in gsel]
            fg0 = fetch("g", "g", jts[0])
            g_tok0 = pass3(gb_[0], wbuf["g"], xh_f, xl_f,
                           [bank_free[0 + 2 * gsel[0]],
                            bank_free[1 + 2 * gsel[0]], fg0])
            wbuf_free["g"] = g_tok0
            fu0 = fetch("u", "u", jts[0])
            u_tok0 = pass3(ub_[0], wbuf["u"], xh_f, xl_f,
                           [bank_free[4 + 2 * gsel[0]],
                            bank_free[5 + 2 * gsel[0]], fu0])
            wbuf_free["u"] = u_tok0
            fg1 = fetch("g", "g", jts[1])
            fu1 = fetch("u", "u", jts[1])
            wbuf_free["d0"] = wd_free
            wbuf_free["d1"] = wd_free
            fd0 = fetch("d0", "d", jts[0])
            fd1 = fetch("d1", "d", jts[1])
            gated_evac(0, gsel[0], gb_[0], ub_[0], g_tok0, u_tok0)
            g_tok1 = pass3(gb_[1], wbuf["g"], xh_f, xl_f,
                           [bank_free[0 + 2 * gsel[1]],
                            bank_free[1 + 2 * gsel[1]], fg1])
            wbuf_free["g"] = g_tok1
            u_tok1 = pass3(ub_[1], wbuf["u"], xh_f, xl_f,
                           [bank_free[4 + 2 * gsel[1]],
                            bank_free[5 + 2 * gsel[1]], fu1])
            wbuf_free["u"] = u_tok1
            gated_evac(1, gsel[1], gb_[1], ub_[1], g_tok1, u_tok1)
            last_down = None
            ev_hist = [None, None]
            jj1_waited = False
            for dt in range(KC):
                dsel = dt % 2
                db = [pb[0 + 2 * dsel], pb[1 + 2 * dsel]]
                if dt < 2:
                    first = ([bank_free[0 + 2 * dsel], bank_free[1 + 2 * dsel]]
                             + list(gated_ready[0]) + [fd0])
                else:
                    first = list(ev_hist[dsel])
                for jj in range(2):
                    wdb = wbuf["d%d" % jj]
                    whi = wdb[:, dt * 128:(dt + 1) * 128]
                    wlo = wdb[:, 2048 + dt * 128:2048 + (dt + 1) * 128]
                    gh = [gated_hi[:, jj * TC + t * 512:
                                   jj * TC + t * 512 + 512] for t in range(2)]
                    gl = [gated_lo[:, jj * TC + t * 512:
                                   jj * TC + t * 512 + 512] for t in range(2)]
                    if jj == 0:
                        w0 = first
                    elif not jj1_waited:
                        w0 = list(gated_ready[1]) + [fd1]
                        jj1_waited = True
                    else:
                        w0 = ()
                    mm(db[0], whi, gh[0], start=(jj == 0), stop=False,
                       waits=w0)
                    mm(db[1], whi, gh[1], start=(jj == 0), stop=False)
                    mm(db[0], whi, gl[0], start=False, stop=False)
                    mm(db[1], whi, gl[1], start=False, stop=False)
                    mm(db[0], wlo, gh[0], start=False, stop=(jj == 1))
                    last_down = mm(db[1], wlo, gh[1], start=False,
                                   stop=(jj == 1),
                                   inc="tm" if jj == 1 else None)
                e0 = v_op(lambda E,
                          o=x_res[:, dt * TC:dt * TC + 512], i=db[0]:
                          E.scalar_tensor_tensor(o, i, 0.5, o, op0=ALU.mult,
                                                 op1=ALU.add),
                          waits=[last_down])
                e1 = v_op(lambda E,
                          o=x_res[:, dt * TC + 512:dt * TC + 1024], i=db[1]:
                          E.scalar_tensor_tensor(o, i, 0.5, o, op0=ALU.mult,
                                                 op1=ALU.add))
                ev_hist[dsel] = (e0, e1)
                bank_free[0 + 2 * dsel] = e0
                bank_free[1 + 2 * dsel] = e1
                if store_out and rnd == n_round - 1:
                    bld.dma("g", out[dt * 128:(dt + 1) * 128, :],
                            x_res[:, dt * TC:(dt + 1) * TC],
                            waits=[e0, e1])
            gated_free[0] = last_down
            gated_free[1] = last_down
            wd_free = last_down
        bld.fence()

    ln_full()
    ffn_phase(build_cache=True)

    ln_toks = ln_full()
    for kc in range(KC):
        hi_t, lo_t = ln_toks[kc]
        bld.dma("g", xb.ap()[kc * 128:(kc + 1) * 128, :],
                xln_hi[:, kc * TC:(kc + 1) * TC], waits=[hi_t, lo_t])
        bld.dma("g", xb.ap()[C + kc * 128:C + (kc + 1) * 128, :],
                xln_lo[:, kc * TC:(kc + 1) * TC])
    bld.queue_barrier("g")
    ag_tok = bld.op("g", lambda E: E.collective_compute(
        "AllGather", ALU.bypass, replica_groups=[list(range(NCORES))],
        ins=[xb.ap()], outs=[xg.ap()]), inc="ccG")

    def qkv_phase():
        NT = 32
        TW = 256
        whl = []
        o = 0
        for m in range(6):
            if m % 3 == 2:
                base = 24576 + (m // 3) * 4096
                whl.append((pool_ln[:, base:base + 2048],
                            pool_ln[:, base + 2048:base + 4096]))
            else:
                whl.append((slot_bf16(o, 2048), slot_bf16(o + 1024, 2048)))
                o += 2048
        xgb = []
        for b_ in range(2):
            xgb.append((slot_bf16(o, KC * TW), slot_bf16(o + KC * TW // 2,
                                                         KC * TW)))
            o += KC * TW
        evacf = [slot_f32(o, TW), slot_f32(o + TW, TW)]; o += 2 * TW
        hb = [slot_bf16(o, TW), slot_bf16(o + TW // 2, TW)]; o += TW
        lb = [slot_bf16(o, TW), slot_bf16(o + TW // 2, TW)]; o += TW
        sqs = slot_f32(o, TW); o += TW
        vt_s = [slot_f32(o, TW), slot_f32(o + TW, TW)]; o += 2 * TW
        assert o <= POOLN, o
        wstage = pool[:, 12288:12288 + 2048]
        wstage3 = wstage.rearrange("p (kc d) -> p kc d", kc=KC)

        bld.fence()
        wsrc = (wqh, wkh, wvh)
        ws_users = [None, None]
        wready = []
        for m in range(6):
            hh, m_ = m // 3, m % 3
            src = wsrc[m_][hh * C:(hh + 1) * C, :]
            src = src.rearrange("(kc p) d -> p kc d", p=128)
            d_tok = bld.dma("s", wstage3, src, waits=ws_users, sem="ws")
            h_tok = a_op(lambda E, oh=whl[m][0], i=wstage:
                         E.activation(oh, i, AF.Copy), waits=[d_tok])
            l_tok = v_op(lambda E, ol=whl[m][1], i=wstage, hx=whl[m][0]:
                         E.tensor_tensor(ol, i, hx, op=ALU.subtract),
                         waits=[h_tok])
            ws_users = [h_tok, l_tok]
            wready.append((h_tok, l_tok))
        init_tok = v_op(lambda E: E.memset(smalls[:, 0:4], 0.0))

        bank_user = {i: None for i in range(8)}
        evac_user = [None, None]
        hl_user = [None, None]
        vt_user = [None, None]
        xg_user = [ws_users[0], ws_users[1]]
        stats_user = None
        nb = 0
        for tt in range(NT):
            r, colh = tt // 4, tt % 4
            b_ = tt % 2
            cs = slice(colh * TW, (colh + 1) * TW)
            src_h = xg.ap()[r * 2 * C:r * 2 * C + C, cs]
            src_l = xg.ap()[r * 2 * C + C:r * 2 * C + 2 * C, cs]
            sem = "xa%d" % b_
            dh_tok = bld.dma("s",
                             xgb[b_][0].rearrange("p (kc n) -> p kc n", kc=KC),
                             src_h.rearrange("(kc p) n -> p kc n", p=128),
                             waits=[ag_tok, xg_user[b_]], sem=sem)
            dl_tok = bld.dma("s",
                             xgb[b_][1].rearrange("p (kc n) -> p kc n", kc=KC),
                             src_l.rearrange("(kc p) n -> p kc n", p=128),
                             sem=sem)
            for h in range(2):
                for m_ in range(3):
                    m = h * 3 + m_
                    banki = (tt % 2) * 2 + h
                    bank = pb[banki][:, 0:TW]
                    first = [dh_tok, dl_tok, wready[m][0], wready[m][1],
                             bank_user[banki], init_tok]
                    tok = None
                    for kc in range(KC):
                        whi = whl[m][0][:, kc * 128:(kc + 1) * 128]
                        wlo = whl[m][1][:, kc * 128:(kc + 1) * 128]
                        xh = xgb[b_][0][:, kc * TW:(kc + 1) * TW]
                        xl = xgb[b_][1][:, kc * TW:(kc + 1) * TW]
                        w0 = first if kc == 0 else ()
                        mm(bank, whi, xh, start=(kc == 0), stop=False, waits=w0)
                        mm(bank, whi, xl, start=False, stop=False)
                        tok = mm(bank, wlo, xh, start=False,
                                 stop=(kc == KC - 1),
                                 inc="tm" if kc == KC - 1 else None)
                    xg_user[b_] = tok
                    bias_ap = (bq_s, bk_s, bv_s)[m_][:, h:h + 1]
                    eb = nb % 2
                    ev = v_op(lambda E, o=evacf[eb], i=bank, ba=bias_ap:
                              E.tensor_scalar_add(o, i, ba),
                              waits=[tok, evac_user[eb]])
                    bank_user[banki] = ev
                    if m_ in (0, 1):
                        dst = qds if m_ == 0 else kds
                        hi_t = a_op(lambda E, o=hb[eb], i=evacf[eb]:
                                    E.activation(o, i, AF.Copy),
                                    waits=[ev, hl_user[eb]])
                        lo_t = v_op(lambda E, o=lb[eb], i=evacf[eb], hx=hb[eb]:
                                    E.tensor_tensor(o, i, hx, op=ALU.subtract),
                                    waits=[hi_t])
                        bld.dma("g", dst.ap()[h * 256:h * 256 + 128,
                                              tt * TW:(tt + 1) * TW],
                                hb[eb], waits=[hi_t], sem="hl%d" % eb)
                        d2 = bld.dma("g",
                                     dst.ap()[h * 256 + 128:h * 256 + 256,
                                              tt * TW:(tt + 1) * TW],
                                     lb[eb], waits=[lo_t], sem="hl%d" % eb)
                        hl_user[eb] = d2
                        sq_t = a_op(lambda E, o=sqs, i=evacf[eb]:
                                    E.activation(o, i, AF.Square), waits=[ev])
                        n2 = mm(pb[6][0:1, 0:TW], onesc, sqs, start=True,
                                stop=True, waits=[sq_t, stats_user], inc="tm")
                        rd = v_op(lambda E, o=smalls[:, 8:9],
                                  i=pb[6][0:1, 0:TW]:
                                  E.reduce_max(o, i, axis=AX), waits=[n2])
                        acc = m_ * 2 + h
                        mx = v_op(lambda E, o=smalls[:, acc:acc + 1],
                                  i=smalls[:, 8:9]: E.tensor_max(o, o, i))
                        stats_user = mx
                        evac_user[eb] = sq_t
                    else:
                        vb = nb % 2
                        tts = None
                        for ti in range(2):
                            tts = tr(pb[7][:, ti * 128:(ti + 1) * 128],
                                     evacf[eb][:, ti * 128:(ti + 1) * 128],
                                     waits=[ev, bank_user[7]] if ti == 0
                                     else (),
                                     inc="tm" if ti == 1 else None)
                        cp = v_op(lambda E, o=vt_s[vb], i=pb[7][:, 0:TW]:
                                  E.tensor_copy(o, i),
                                  waits=[tts, vt_user[vb]])
                        bank_user[7] = cp
                        dv = bld.dma("g",
                                     vds.ap()[h * T + tt * TW:
                                              h * T + (tt + 1) * TW, :]
                                     .rearrange("(a p) d -> p a d", p=128),
                                     vt_s[vb].rearrange("p (a d) -> p a d",
                                                        a=2),
                                     waits=[cp], sem="vt%d" % vb)
                        vt_user[vb] = dv
                        evac_user[eb] = cp
                    nb += 1
        bc_user = None
        for h in range(2):
            pr = v_op(lambda E, o=smalls[:, 16 + h:17 + h],
                      i0=smalls[:, 0 + h:1 + h], i1=smalls[:, 2 + h:3 + h]:
                      E.tensor_mul(o, i0, i1), waits=[stats_user])
            sq_t = a_op(lambda E, o=smalls[:, 16 + h:17 + h],
                        i=smalls[:, 16 + h:17 + h]:
                        E.activation(o, i, AF.Sqrt), waits=[pr])
            ng = v_op(lambda E, o=smalls[:, 16 + h:17 + h],
                      i=smalls[:, 16 + h:17 + h]:
                      E.tensor_scalar_mul(o, i, -ATT_SCALE), waits=[sq_t])
            bc = mm(pb[6][:, 0:1], onesr, smalls[:, 16 + h:17 + h],
                    start=True, stop=True, waits=[ng, bc_user], inc="tm")
            bc_user = v_op(lambda E, o=bneg_s[:, h:h + 1], i=pb[6][:, 0:1]:
                           E.tensor_copy(o, i), waits=[bc])
        bld.fence()

    qkv_phase()

    def attn_phase():
        mt_tok = bld.dma("s", maskt.rearrange("p (a n) -> p a n", a=4),
                         masktD.rearrange("(a p) n -> p a n", p=128))
        bld.wait("t", mt_tok)
        o = 0
        qh_s = [slot_bf16(o, 512), slot_bf16(o + 256, 512)]; o += 512
        ql_s = [slot_bf16(o, 512), slot_bf16(o + 256, 512)]; o += 512
        kh_s = [slot_bf16(o, 512), slot_bf16(o + 256, 512)]; o += 512
        kl_s = [slot_bf16(o, 512), slot_bf16(o + 256, 512)]; o += 512
        vt_l = [slot_f32(o, 512), slot_f32(o + 512, 512)]; o += 1024
        P_s = [slot_f32(o + i * 512, 512) for i in range(8)]; o += 8 * 512
        recipb = slot_f32(o, 512); o += 512
        yf = slot_f32(o, 512); o += 512
        recr = pool[0:1, o:o + 512]; o += 512
        cbuf = slot_f32(o, 512); o += 512
        assert o <= POOLN

        y_hi = pool_ln[:, 0:KC * TC]
        y_lo = pool_ln[:, KC * TC:2 * KC * TC]
        scpb = [pb[0], pb[1], pb[5], pb[6]]

        q_user = [None, None]
        k_user = [None, None]
        v_user = [None, None]
        p_user = [None] * 8
        sc_user = [None] * 4
        y_user = None
        sums_user = None
        rb_user = None
        yb_user = None
        np_ = 0

        def pv_sums_batch(batch, sb, qc):
            toks = []
            for j, (pi, pvt, p_tok, ptv) in enumerate(batch):
                w1 = [p_tok, ptv]
                if sb == 0 and j == 0:
                    w1 += [y_user, sums_user]
                mm(pb[2], pvt, P_s[pi], start=(sb == 0 and j == 0),
                   stop=(sb == qc and j == len(batch) - 1), waits=w1)
            for j, (pi, pvt, p_tok, ptv) in enumerate(batch):
                tok = mm(pb[3][32 * j:32 * j + 1, :], onesc, P_s[pi],
                         start=(sb == 0), stop=(sb == qc),
                         inc="tm" if j == len(batch) - 1 else None,
                         tile_position=(0, 32 * j))
                if tok:
                    for pi2, _, _, _ in batch:
                        p_user[pi2] = tok
            return tok

        for h in range(2):
            for qc in range(16):
                qb = qc % 2
                tq = bld.dma("s", qh_s[qb],
                             qds.ap()[h * 256:h * 256 + 128,
                                      qc * 512:(qc + 1) * 512],
                             waits=[q_user[qb]], sem="q%d" % qb)
                tq2 = bld.dma("s", ql_s[qb],
                              qds.ap()[h * 256 + 128:h * 256 + 256,
                                       qc * 512:(qc + 1) * 512],
                              sem="q%d" % qb)
                pend = None
                kq_last = {}
                v_last = {}
                for sb in range(qc + 1):
                    kb = sb % 2
                    k_user[kb] = kq_last.get(kb, k_user[kb])
                    v_user[kb] = v_last.get(kb, v_user[kb])
                    tk = bld.dma("s", kh_s[kb],
                                 kds.ap()[h * 256:h * 256 + 128,
                                          sb * 512:(sb + 1) * 512],
                                 waits=[k_user[kb]], sem="k%d" % kb)
                    tk2 = bld.dma("s", kl_s[kb],
                                  kds.ap()[h * 256 + 128:h * 256 + 256,
                                           sb * 512:(sb + 1) * 512],
                                  sem="k%d" % kb)
                    vb = sb % 2
                    tv = bld.dma("g",
                                 vt_l[vb].rearrange("p (a d) -> p a d", a=4),
                                 vds.ap()[h * T + sb * 512:
                                          h * T + (sb + 1) * 512, :]
                                 .rearrange("(a p) d -> p a d", p=128),
                                 waits=[v_user[vb]], sem="v%d" % vb)
                    batch = []
                    for si in range(4):
                        spb = scpb[si]
                        khi = kh_s[kb][:, si * 128:(si + 1) * 128]
                        klo = kl_s[kb][:, si * 128:(si + 1) * 128]
                        w0 = [tk, tk2, tq, tq2, sc_user[si]]
                        mm(spb, khi, qh_s[qb], start=True, stop=False,
                           waits=w0)
                        mm(spb, khi, ql_s[qb], start=False, stop=False)
                        qk_tok = mm(spb, klo, qh_s[qb], start=False,
                                    stop=True, inc="tm")
                        kq_last[kb] = qk_tok
                        if sb == qc:
                            mk = maskt[:, si * 512:(si + 1) * 512]
                            qk_tok = v_op(lambda E, o=spb, i=spb, mkk=mk:
                                          E.tensor_tensor(o, i, mkk,
                                                          op=ALU.add),
                                          waits=[qk_tok])
                        pi = np_ % 8
                        np_ += 1
                        p_tok = a_op(lambda E, o=P_s[pi], i=spb,
                                     ba=bneg_s[:, h:h + 1]:
                                     E.activation(o, i, AF.Exp, bias=ba,
                                                  scale=ATT_SCALE),
                                     waits=[qk_tok, p_user[pi]])
                        sc_user[si] = p_tok
                        batch.append((pi, vt_l[vb][:, si * 128:(si + 1) * 128],
                                      p_tok, tv))
                    if pend is not None:
                        tokp = pv_sums_batch(pend[0], pend[1], qc)
                        v_last[pend[1] % 2] = tokp
                    pend = (batch, sb)
                tokp = pv_sums_batch(pend[0], pend[1], qc)
                v_last[pend[1] % 2] = tokp
                k_user[0] = kq_last.get(0, k_user[0])
                k_user[1] = kq_last.get(1, k_user[1])
                v_user[0] = v_last.get(0, v_user[0])
                v_user[1] = v_last.get(1, v_user[1])
                q_user[qb] = tokp
                done_tok = tokp
                cp4 = v_op(lambda E: E.tensor_copy(cbuf[0:97, :],
                                                   pb[3][0:97, :]),
                           waits=[done_tok, rb_user])
                sums_user = cp4
                tot = mm(pb[4][0:1, :], sum4, cbuf[0:97, :], start=True,
                         stop=True, waits=[cp4, rb_user], inc="tm")
                rc = v_op(lambda E: E.reciprocal(recr, pb[4][0:1, :]),
                          waits=[tot])
                bc = mm(pb[4], onesr, recr, start=True, stop=True,
                        waits=[rc], inc="tm")
                cb = v_op(lambda E: E.tensor_copy(recipb, pb[4]), waits=[bc])
                rb_user = cb
                ys = v_op(lambda E: E.tensor_tensor(yf, pb[2], recipb,
                                                    op=ALU.mult),
                          waits=[cb, yb_user])
                y_user = ys
                dsty_h = y_hi[:, h * T + qc * 512: h * T + (qc + 1) * 512]
                dsty_l = y_lo[:, h * T + qc * 512: h * T + (qc + 1) * 512]
                yh_t = a_op(lambda E, o=dsty_h, i=yf:
                            E.activation(o, i, AF.Copy), waits=[ys])
                yb_user = v_op(lambda E, o=dsty_l, i=yf, hx=dsty_h:
                               E.tensor_tensor(o, i, hx, op=ALU.subtract),
                               waits=[yh_t])
        bld.fence()

    attn_phase()

    def wo_phase():
        wstage = slot_f32(3 * TC, 2048)
        o = 3 * TC + 2048
        wo_hi = slot_bf16(o, 2 * 2048); o += 2048
        wo_lo = slot_bf16(o, 2 * 2048); o += 2048
        wev = [slot_f32(o, 512), slot_f32(o + 512, 512)]; o += 1024
        rsb = slot_f32(o, 512); o += 512
        tmp = slot_f32(o, 512); o += 512
        assert o <= POOLN
        y_hi = pool_ln[:, 0:KC * TC]
        y_lo = pool_ln[:, KC * TC:2 * KC * TC]

        ws_users = [None, None]
        wr = []
        for cc in range(2):
            d_tok = bld.dma("s", wstage, woh[cc * 128:(cc + 1) * 128, :],
                            waits=ws_users, sem="ws")
            h_tok = a_op(lambda E, oh=wo_hi[:, cc * 2048:(cc + 1) * 2048],
                         i=wstage: E.activation(oh, i, AF.Copy),
                         waits=[d_tok])
            l_tok = v_op(lambda E, ol=wo_lo[:, cc * 2048:(cc + 1) * 2048],
                         i=wstage, hx=wo_hi[:, cc * 2048:(cc + 1) * 2048]:
                         E.tensor_tensor(ol, i, hx, op=ALU.subtract),
                         waits=[h_tok])
            ws_users = [h_tok, l_tok]
            wr.append((h_tok, l_tok))

        bank_user = [None, None]
        ev_user = [None, None]
        rs_toks = []
        xadd_state = [None, None]

        def x2_add(half):
            d_user, t_user = xadd_state
            for kc in range(KC):
                dr = bld.dma("s", rsb,
                             rsH[half].ap()[kc * 128:(kc + 1) * 128, :],
                             waits=[rs_toks[half], d_user], sem="rs")
                at = a_op(lambda E, o=tmp, i=rsb, ba=bo_s[:, kc:kc + 1]:
                          E.activation(o, i, AF.Identity, bias=ba),
                          waits=[dr, t_user])
                d_user = at
                t_user = v_op(lambda E,
                              o=x_res[:, kc * TC + half * 512:
                                      kc * TC + half * 512 + 512], i=tmp:
                              E.tensor_tensor(o, o, i, op=ALU.add),
                              waits=[at])
            xadd_state[0] = d_user
            xadd_state[1] = t_user

        for half in range(2):
            for r in range(NCORES):
                tt = 2 * r + half
                for dt in range(KC):
                    bi = (r * KC + dt) % 2
                    bank = pb[bi]
                    first = [bank_user[bi]] + list(wr[0]) + list(wr[1])
                    last = None
                    for cc in range(2):
                        whi = wo_hi[:, cc * 2048 + dt * 128:
                                    cc * 2048 + (dt + 1) * 128]
                        wlo = wo_lo[:, cc * 2048 + dt * 128:
                                    cc * 2048 + (dt + 1) * 128]
                        yh = y_hi[:, cc * T + tt * 512:cc * T + (tt + 1) * 512]
                        yl = y_lo[:, cc * T + tt * 512:cc * T + (tt + 1) * 512]
                        w0 = first if cc == 0 else ()
                        mm(bank, whi, yh, start=(cc == 0), stop=False,
                           waits=w0)
                        mm(bank, whi, yl, start=False, stop=False)
                        last = mm(bank, wlo, yh, start=False, stop=(cc == 1),
                                  inc="tm" if cc == 1 else None)
                    ev = v_op(lambda E, o=wev[bi], i=bank:
                              E.tensor_copy(o, i), waits=[last, ev_user[bi]])
                    bank_user[bi] = ev
                    ev_user[bi] = bld.dma(
                        "g",
                        wopH[half].ap()[r * C + dt * 128:
                                        r * C + (dt + 1) * 128, :],
                        wev[bi], waits=[ev], sem="wv%d" % bi)
            bld.queue_barrier("g")
            rs_toks.append(bld.op("g", lambda E, i=wopH[half], oo=rsH[half]:
                                  E.collective_compute(
                                      "ReduceScatter", ALU.add,
                                      replica_groups=[list(range(NCORES))],
                                      ins=[i.ap()], outs=[oo.ap()]),
                                  inc=("ccA", "ccB")[half]))
            if half == 1:
                x2_add(0)
                x2_add(1)
        bld.fence()

    wo_phase()

    ln_full()
    ffn_phase(build_cache=False, store_out=True)
    bld.queue_barrier("g")

    with nc.Block() as block:
        bld.emit(block)
    return nc


_NC_CACHE = [None]


def kernel(L0_attn_x, L0_attn_Wq, L0_attn_bq, L0_attn_Wk, L0_attn_bk,
           L0_attn_Wv, L0_attn_bv, L0_attn_Wo, L0_attn_bo,
           L0_ffn_w_gate, L0_ffn_w_up, L0_ffn_w_down):
    f32 = np.float32
    x2d = np.asarray(L0_attn_x, f32).reshape(T, C)
    xT = np.ascontiguousarray(x2d.T)
    def jt_tiled(wT):
        r = wT.reshape(KC, 128, NJT, 128).transpose(2, 1, 0, 3)
        return np.ascontiguousarray(r.reshape(NJT * 128, KC * 128))

    wg_h = jt_tiled(np.asarray(L0_ffn_w_gate, f32).T)
    wu_h = jt_tiled(np.asarray(L0_ffn_w_up, f32).T)
    wd_h = np.ascontiguousarray(np.asarray(L0_ffn_w_down, f32).T)
    Wq = np.asarray(L0_attn_Wq, f32)
    Wk = np.asarray(L0_attn_Wk, f32)
    Wv = np.asarray(L0_attn_Wv, f32)
    Wo = np.asarray(L0_attn_Wo, f32)
    bq = np.asarray(L0_attn_bq, f32)
    bk = np.asarray(L0_attn_bk, f32)
    bv = np.asarray(L0_attn_bv, f32)
    bo = np.asarray(L0_attn_bo, f32)

    identc = np.eye(128, dtype=f32)
    onescc = np.ones((128, 1), dtype=f32)
    onesrc = np.ones((1, 128), dtype=f32)
    s_idx = np.arange(512)[:, None]
    t_idx = np.arange(512)[None, :]
    maskc = np.where(s_idx <= t_idx, 0.0, NEG_BIG).astype(ml_dtypes.bfloat16)
    boTc = np.ascontiguousarray(bo.reshape(KC, 128).T)
    sum4c = np.zeros((97, 1), dtype=f32)
    sum4c[[0, 32, 64, 96], 0] = 1.0

    in_maps = []
    for c in range(NCORES):
        h0, h1 = 2 * c, 2 * c + 1
        im = {
            "xin": np.ascontiguousarray(xT[:, c * TC:(c + 1) * TC]),
            "wg": wg_h, "wu": wu_h, "wd": wd_h,
            "wqh": np.ascontiguousarray(
                np.concatenate([Wq[h0], Wq[h1]], axis=0)),
            "wkh": np.ascontiguousarray(
                np.concatenate([Wk[h0], Wk[h1]], axis=0)),
            "wvh": np.ascontiguousarray(
                np.concatenate([Wv[h0], Wv[h1]], axis=0)),
            "woh": np.ascontiguousarray(Wo[h0 * DH:(h1 + 1) * DH, :]),
            "bqh": np.ascontiguousarray(np.stack([bq[h0], bq[h1]], axis=1)),
            "bkh": np.ascontiguousarray(np.stack([bk[h0], bk[h1]], axis=1)),
            "bvh": np.ascontiguousarray(np.stack([bv[h0], bv[h1]], axis=1)),
            "boT": boTc,
            "ident": identc, "onesc": onescc, "onesr": onesrc,
            "maskt": maskc, "sum4": sum4c,
        }
        in_maps.append(im)

    if _NC_CACHE[0] is None:
        _NC_CACHE[0] = build_program()
    nc = _NC_CACHE[0]
    res = run_bass_kernel_spmd(nc, in_maps, core_ids=list(range(NCORES)))

    y = np.empty((T, C), dtype=f32)
    for c in range(NCORES):
        o = res.results[c]["out"]
        y[c * TC:(c + 1) * TC, :] = o.T
    return y.reshape(1, T, C)
